# version 02 (final)
# baseline (speedup 1.0000x reference)
import numpy as np
import ml_dtypes
from contextlib import ExitStack
import concourse.bass as bass
import concourse.mybir as mybir
from concourse.bass_utils import run_bass_kernel_spmd
from concourse.alu_op_type import AluOpType as ALU

F32 = mybir.dt.float32
BF16 = mybir.dt.bfloat16
AF = mybir.ActivationFunctionType
BF = ml_dtypes.bfloat16

B, S, D = 8, 2048, 1024
NH, HD = 16, 64
NCORES = 8
INV_SCALE = float(np.float32(1.0) / np.sqrt(np.float32(D // NH * 3)))
LN_EPS_PRE = 1e-5
LN_EPS_ATT = 1e-7

def _bucket_np(delta):
    mid = 128
    MAX_REL = 512
    sign = np.sign(delta).astype(np.int32)
    abs_pos = np.where((delta < mid) & (delta > -mid), mid - 1, np.abs(delta)).astype(np.int32)
    ratio = np.log(abs_pos.astype(np.float32) / np.float32(mid)).astype(np.float32) / np.float32(
        np.log(np.float32((MAX_REL - 1) / mid), dtype=np.float32))
    log_pos = (np.ceil((ratio * np.float32(mid - 1)).astype(np.float32)).astype(np.float32)
               + np.float32(mid)).astype(np.float32)
    bucket = np.where(abs_pos <= mid, abs_pos.astype(np.float32), log_pos * sign.astype(np.float32))
    return bucket.astype(np.int32)

_DELTA = np.arange(-(S - 1), S)
_BKT = _bucket_np(_DELTA)
G_C2P = np.clip(_BKT + 256, 0, 511)
G_P2C = np.clip(-_BKT + 256, 0, 511)

SELW = 1536

def _sel(G):
    M = np.zeros((512, SELW), dtype=np.float32)
    for jp in range(SELW):
        d = 767 - jp
        M[G[d + 2047], jp] = 1.0
    return M.astype(BF)

SELC = _sel(G_C2P)
SELC2 = _sel(G_P2C)

ENGS = ("sync", "tensor", "scalar", "vector", "gpsimd")

DGPOOL = ["dg%d" % i for i in range(8)]
DSPOOL = ["dt%d" % i for i in range(4)]
ENG_KEY = {"tensor": "t", "scalar": "a", "vector": "v", "gpsimd": "g", "sync": "s"}

class P:
    def __init__(self):
        self.nc = bass.Bass("TRN2", target_bir_lowering=False)
        self.es = ExitStack()
        self.steps = {e: [] for e in ENGS}
        self.cnt = {}
        self.sems = {}
        for k in ("t", "a", "v", "g", "s", "mt", "ma", "mv", "mg", "ms") + tuple(DGPOOL) + tuple(DSPOOL):
            self.sems[k] = self.es.enter_context(self.nc.semaphore("sem_" + k))
            self.cnt[k] = 0
        self.slot_rr = {"g": 0, "s": 0}
        self.slot_guards = {k: [] for k in DGPOOL + DSPOOL}
        self._tensors = {}

    def sb(self, name, shape, dt=F32):
        t = self.es.enter_context(self.nc.sbuf_tensor(name, list(shape), dt))
        self._tensors[name] = t
        return t

    def ps(self, name, shape, dt=F32):
        t = self.es.enter_context(self.nc.psum_tensor(name, list(shape), dt))
        self._tensors[name] = t
        return t

    def dram(self, name, shape, dt=F32, kind="Internal"):
        t = self.nc.dram_tensor(name, list(shape), dt, kind=kind)
        self._tensors[name] = t
        return t.ap()

    def wait(self, eng, tok):
        if tok is None:
            return
        sk, v = tok
        if sk in self.slot_guards:
            v = self.cnt[sk]
        if v <= 0:
            return
        sem = self.sems[sk]
        self.steps[eng].append(lambda e, sem=sem, v=v: e.wait_ge(sem, v))
        if sk in self.slot_guards:
            mk = "m" + ENG_KEY[eng]
            msem = self.sems[mk]
            self.cnt[mk] += 1
            nv = self.cnt[mk]
            self.steps[eng].append(lambda e, msem=msem: e.sem_inc(msem, 1))
            self.slot_guards[sk].append((mk, nv))

    def waits(self, eng, toks):
        best = {}
        for t in toks:
            if t is None:
                continue
            sk, v = t
            best[sk] = max(best.get(sk, 0), v)
        for sk, v in best.items():
            self.wait(eng, (sk, v))

    SELF_SYNC = {"scalar": "a", "vector": "v", "gpsimd": "g"}

    def op(self, eng, build, inc=None, n=1):
        sk = self.SELF_SYNC.get(eng)
        if sk is not None and inc != sk:
            pass
        if sk is not None and self.cnt[sk] > 0:
            sem = self.sems[sk]
            v = self.cnt[sk]
            self.steps[eng].append(lambda e, sem=sem, v=v: e.wait_ge(sem, v))
        if inc is not None:
            self.cnt[inc] += n
            tokv = self.cnt[inc]
            sem = self.sems[inc]
            self.steps[eng].append(lambda e, b=build, s=sem, n=n: b(e).then_inc(s, n))
            return (inc, tokv)
        self.steps[eng].append(lambda e, b=build: b(e))
        return None

    def tok(self, key):
        return (key, self.cnt[key])

    def dma_slot(self, queue):
        pool = DGPOOL if queue == "g" else DSPOOL
        i = self.slot_rr[queue]; self.slot_rr[queue] = i + 1
        key = pool[i % len(pool)]
        return key

    def all_dma_toks(self):
        return [(k, self.cnt[k]) for k in DGPOOL + DSPOOL]

    def finalize(self):
        nc = self.nc
        with self.es, nc.Block() as block:
            def run(steps):
                def f(e):
                    for s in steps:
                        s(e)
                return f
            block.sync(run(self.steps["sync"]))
            block.tensor(run(self.steps["tensor"]))
            block.scalar(run(self.steps["scalar"]))
            block.vector(run(self.steps["vector"]))
            block.gpsimd(run(self.steps["gpsimd"]))
        return nc

EMAP = {"t": "tensor", "a": "scalar", "v": "vector", "g": "gpsimd", "s": "sync"}

def mm(p, out, lhsT, rhs, start=True, stop=True, inc="t", **kw):
    return p.op("tensor", lambda e: e.matmul(out, lhsT, rhs, start=start, stop=stop, **kw),
                inc=inc)

def transpose(p, out, in_, ident, inc="t"):
    return p.op("tensor", lambda e: e.transpose(out, in_, ident), inc=inc)

def dma(p, eng, out, in_, inc=None, **kw):
    q = "g" if eng == "g" else "s"
    key = p.dma_slot(q)
    ename = EMAP[eng]
    for gd in p.slot_guards[key]:
        p.wait(ename, gd)
    p.slot_guards[key] = []
    return p.op(ename, lambda e: e.dma_start(out=out, in_=in_, **kw), inc=key, n=16)

def act(p, out, in_, func=AF.Identity, bias=None, scale=None, inc="a"):
    kw = {}
    if bias is not None:
        kw["bias"] = bias
    if scale is not None:
        kw["scale"] = scale
    return p.op("scalar", lambda e: e.activation(out, in_, func, **kw), inc=inc)

def tt(p, out, a, b, opk=ALU.add, inc="v"):
    return p.op("vector", lambda e: e.tensor_tensor(out=out, in0=a, in1=b, op=opk), inc=inc)

def ts(p, out, a, scalar, opk=ALU.mult, inc="v"):
    return p.op("vector", lambda e: e.tensor_scalar(out=out, in0=a, scalar1=scalar, scalar2=None, op0=opk), inc=inc)

def stt(p, out, in0, scalar, in1, op0=ALU.mult, op1=ALU.add, inc="v"):
    return p.op("vector", lambda e: e.scalar_tensor_tensor(out=out, in0=in0, scalar=scalar, in1=in1, op0=op0, op1=op1), inc=inc)

def vcopy(p, out, in_, inc="v"):
    return p.op("vector", lambda e: e.tensor_copy(out, in_), inc=inc)

def vred(p, out, in_, opk=ALU.max, inc="v"):
    return p.op("vector", lambda e: e.tensor_reduce(out=out, in_=in_, axis=mybir.AxisListType.X, op=opk), inc=inc)

def recip(p, out, in_, inc="v"):
    return p.op("vector", lambda e: e.reciprocal(out=out, in_=in_), inc=inc)

def barrier(p):
    toks = [p.tok(k) for k in ("t", "a", "v", "g", "s")] + p.all_dma_toks()
    for eng in ENGS:
        for t_ in toks:
            if t_[1] > 0:
                sem = p.sems[t_[0]]; v = t_[1]
                p.steps[eng].append(lambda e, sem=sem, v=v: e.wait_ge(sem, v))
    for k in p.slot_guards:
        p.slot_guards[k] = []
    return toks

def build_l0(upto="full"):
    p = P()

    x_in = p.dram("x_in", (S, D), F32, kind="ExternalInput")
    wq = p.dram("wq", (D, D), F32, kind="ExternalInput")
    wk = p.dram("wk", (D, D), F32, kind="ExternalInput")
    wv = p.dram("wv", (D, D), F32, kind="ExternalInput")
    wo = p.dram("wo", (D, D), F32, kind="ExternalInput")
    bias_in = {n: p.dram(n + "_fm", (128, 8), F32, kind="ExternalInput")
               for n in ("bq", "bk", "bv", "bo", "g", "be", "b2")}
    rel = p.dram("rel", (512, D), F32, kind="ExternalInput")
    selc = p.dram("selc", (512, SELW), BF16, kind="ExternalInput")
    selc2 = p.dram("selc2", (512, SELW), BF16, kind="ExternalInput")
    identb = p.dram("identb", (128, 128), BF16, kind="ExternalInput")
    identf = p.dram("identf", (128, 128), F32, kind="ExternalInput")
    onesrow_b = p.dram("onesrow_b", (1, 512), BF16, kind="ExternalInput")
    onescol_b = p.dram("onescol_b", (128, 1), BF16, kind="ExternalInput")
    onesrow_f = p.dram("onesrow_f", (1, 128), F32, kind="ExternalInput")
    sel64a = p.dram("sel64a", (1, 128), F32, kind="ExternalInput")
    sel64b = p.dram("sel64b", (1, 128), F32, kind="ExternalInput")
    w1t = p.dram("w1t", (D, 4096), F32, kind="ExternalInput")
    b1_fm = p.dram("b1_fm", (128, 32), F32, kind="ExternalInput")
    w2t = p.dram("w2t", (4096, D), F32, kind="ExternalInput")
    scal_sc = p.dram("scal_sc", (1, 1), F32, kind="ExternalInput")

    x_out = p.dram("x_out", (S, D), F32, kind="ExternalOutput")
    dbg = p.dram("dbg", (128, 2048), F32, kind="ExternalOutput")

    xd0 = p.dram("xd0", (8, 128, S), F32)
    xd1 = p.dram("xd1", (8, 128, S), F32)
    ao_d = p.dram("ao_d", (8, 128, S), F32)
    cb_d = p.dram("cb_d", (16, 128 * SELW + 2048), BF16)
    pb_d = p.dram("pb_d", (128 * SELW + 2048,), BF16)
    ctx_d = p.dram("ctx_d", (8, 128, S), BF16)
    row_d = p.dram("row_d", (4096,), F32)
    lrow_d = p.dram("lrow_d", (16, S), F32)
    er_d = p.dram("er_d", (8, 1024), BF16)

    NBF = 20
    bb = [p.sb(f"bb{i}", (128, 2048), BF16) for i in range(NBF)]
    NF = 7
    bf = [p.sb(f"bf{i}", (128, 2048), F32) for i in range(NF)]

    selbuf = p.sb("selbuf", (128, SELW), BF16)
    
    wbuf = [p.sb(f"wb{i}", (128, 1024), BF16) for i in range(3)]
    s_ident = p.sb("s_ident", (128, 128), F32)
    s_identb = p.sb("s_identb", (128, 128), BF16)
    s_or_b = p.sb("s_or_b", (1, 512), BF16)
    s_oc_b = p.sb("s_oc_b", (128, 1), BF16)
    s_or_f = p.sb("s_or_f", (1, 128), F32)
    s_s64a = p.sb("s_s64a", (1, 128), F32)
    s_s64b = p.sb("s_s64b", (1, 128), F32)
    s_bias = {n: p.sb("s_" + n, (128, 8), F32) for n in ("bq", "bk", "bv", "bo", "g", "be", "b2")}
    s_b1 = p.sb("s_b1", (128, 32), F32)
    s_scal = p.sb("s_scal", (1, 1), F32)
    s_scalb = p.sb("s_scalb", (1, 1), BF16)
    s_posta = p.sb("s_posta", (64, 2), BF16)
    s_vt = p.sb("s_vt", (128, 65), BF16)
    s_a0 = p.sb("s_a0", (1, 2048), BF16)
    s_a511 = p.sb("s_a511", (1, 2048), BF16)
    s_pm0 = p.sb("s_pm0", (1, 2048), BF16)
    s_pm511 = p.sb("s_pm511", (1, 2048), BF16)
    s_cols = p.sb("s_cols", (128, 80), F32)
    c_sum = p.sb("c_sum", (128, 16), F32)
    c_sq = p.sb("c_sq", (128, 16), F32)
    c_mu = p.sb("c_mu", (128, 16), F32)
    c_a = p.sb("c_a", (128, 16), F32)
    c_y = p.sb("c_y", (128, 16), F32)
    c_r = p.sb("c_r", (128, 16), F32)
    c_o = p.sb("c_o", (128, 16), F32)
    s_pkt = [p.sb(f"s_pkt{i}", (128, 64), BF16) for i in range(4)]
    s_tile = p.sb("s_tile", (128, 128), F32)
    s_sk = [p.sb(f"s_sk{i}", (128, 128), BF16) for i in range(11)]
    s_er = [p.sb(f"s_er{i}", (1, 1024), BF16) for i in range(2)]
    s_e4 = [p.sb(f"s_e4{i}", (4, 1024), BF16) for i in range(2)]
    s_u = p.sb("s_u", (16, 1024), BF16)
    s_u4 = p.sb("s_u4", (4, 1024), BF16)
    s_ei = [p.sb(f"s_ei{i}", (4, 128), BF16) for i in range(8)]

    class _RowView:
        def __init__(self, t):
            self.t = t
        def __getitem__(self, idx):
            return self.t[idx]
    s_row1 = _RowView(bf[2])
    s_row2 = _RowView(bf[3])
    pA = p.ps("pA", (128, 2048), F32)
    pB = p.ps("pB", (128, 1024), F32)
    pC = p.ps("pC", (128, 1024), BF16)
    pD = p.ps("pD", (128, 512), F32)

    XLN = list(range(0, 8))
    RELB = [8, 9]
    POSK = [10, 11]
    PWC, PW2C = 12, 13
    CHROW = 14
    PHAT = 15
    P2CSK = 16
    PTT = 17
    QP, KP = 18, 19
    VP = 8
    SCR = 9
    wob = [bb[10 + i // 2][:, 1024 * (i % 2):1024 * (i % 2 + 1)] for i in range(8)]


    for (dst, src) in [(s_ident[:], identf), (s_identb[:], identb), (s_or_b[:], onesrow_b),
                       (s_oc_b[:], onescol_b), (s_or_f[:], onesrow_f),
                       (s_s64a[:], sel64a), (s_s64b[:], sel64b), (s_b1[:], b1_fm),
                       (s_scal[:], scal_sc)]:
        dma(p, "s", dst, src)
    for n in s_bias:
        dma(p, "s", s_bias[n][:], bias_in[n])
    dma(p, "g", s_scalb[:], scal_sc)
    p.op("vector", lambda e: e.memset(s_vt[:, 64:65], 1.0), inc="v")
    barrier(p)

    rowd_last = [None]

    def row_to_cols(row_sb, col_off):
        p.wait("sync", rowd_last[0])
        d1 = dma(p, "s", row_d[0:2048].unsqueeze(0), row_sb[0:1, :])
        p.wait("sync", d1)
        d2 = dma(p, "s", s_cols[:, col_off:col_off + 16], row_d[0:2048].rearrange("(q i) -> q i", i=16))
        rowd_last[0] = d2
        return d2

    def cols_to_row(col_off, row_sb):
        p.wait("sync", rowd_last[0])
        d1 = dma(p, "s", row_d[0:2048].rearrange("(q i) -> q i", i=16), s_cols[:, col_off:col_off + 16])
        p.wait("sync", d1)
        d2 = dma(p, "s", row_sb[0:1, :], row_d[0:2048].unsqueeze(0))
        rowd_last[0] = d2
        return d2

    def load_fm_chunk(src_tokmajor, c, dst_f32):
        toks = []
        for i in range(16):
            d1 = dma(p, "s", s_tile[:], src_tokmajor[128 * i:128 * (i + 1), 128 * c:128 * (c + 1)])
            p.wait("tensor", d1)
            tr = transpose(p, pD[:, 0:128], s_tile[:], s_ident[:])
            p.wait("vector", tr)
            ev = vcopy(p, dst_f32[:, 128 * i:128 * (i + 1)], pD[:, 0:128])
            p.wait("tensor", ev)
            p.wait("sync", ev)
            toks.append(ev)
        return toks[-1]

    def ln_stats(src_d, eps):
        for c in range(8):
            lt = dma(p, "s", bf[2][:], src_d[c])
            p.wait("gpsimd", lt)
            ct = dma(p, "g", bb[XLN[c]][:], bf[2][:])
            p.wait("sync", ct)
        barrier(p)
        for tb in range(4):
            for c in range(8):
                t = mm(p, pA[0:1, 512 * tb:512 * (tb + 1)], s_oc_b[:], bb[XLN[c]][:, 512 * tb:512 * (tb + 1)],
                       start=(c == 0), stop=(c == 7))
        p.wait("scalar", t)
        e1 = act(p, s_row1[0:1, :], pA[0:1, :], AF.Copy)
        p.wait("tensor", e1)
        last = None
        for tb in range(4):
            for c in range(8):
                sq = act(p, bb[SCR][:, 0:512], bb[XLN[c]][:, 512 * tb:512 * (tb + 1)], AF.Square)
                p.wait("tensor", sq)
                last = mm(p, pA[0:1, 512 * tb:512 * (tb + 1)], s_oc_b[:], bb[SCR][:, 0:512],
                          start=(c == 0), stop=(c == 7))
                p.wait("scalar", last)
        e2 = act(p, s_row2[0:1, :], pA[0:1, :], AF.Copy)
        p.wait("sync", e1); p.wait("sync", e2)
        row_to_cols(s_row1, 0)
        d2 = row_to_cols(s_row2, 16)
        p.wait("vector", d2)
        vcopy(p, c_sum[:], s_cols[:, 0:16])
        vcopy(p, c_sq[:], s_cols[:, 16:32])
        ts(p, c_mu[:], c_sum[:], 1.0 / D)
        ts(p, c_a[:], c_sq[:], 1.0 / D)
        tt(p, c_y[:], c_mu[:], c_mu[:], ALU.mult)
        tt(p, c_a[:], c_a[:], c_y[:], ALU.subtract)
        tv = ts(p, c_a[:], c_a[:], float(eps), ALU.add)
        p.wait("scalar", tv)
        y0 = act(p, c_y[:], c_a[:], AF.Ln)
        rv = act(p, c_r[:], c_y[:], AF.Exp, scale=-0.5)
        p.wait("vector", rv)
        cm = vcopy(p, s_cols[:, 32:48], c_mu[:])
        rv2 = vcopy(p, s_cols[:, 64:80], c_r[:])
        p.wait("sync", rv2); p.wait("sync", cm)
        if getattr(ln_stats, "_dump_cols", False):
            cv = vcopy(p, bf[4][:, 0:80], s_cols[:])
            p.wait("sync", cv)
            dma(p, "s", dbg[:, 0:80], bf[4][:, 0:80])
        cols_to_row(32, s_row1)
        d4 = cols_to_row(64, s_row2)
        p.wait("tensor", d4)
        for (rowsb, dst) in ((s_row1, bf[0]), (s_row2, bf[1])):
            t = None
            for nb in range(4):
                t = mm(p, pA[:, 512 * nb:512 * (nb + 1)], s_or_f[:], rowsb[0:1, 512 * nb:512 * (nb + 1)])
            p.wait("vector", t)
            ev = vcopy(p, dst[:], pA[:])
            p.wait("tensor", ev)
        barrier(p)

    def ln_apply_to_bf(src_d):
        for c in range(8):
            lt = dma(p, "s", bf[2][:], src_d[c])
            p.wait("vector", lt)
            tt(p, bf[3][:], bf[2][:], bf[0][:], ALU.subtract)
            ev = tt(p, bb[XLN[c]][:], bf[3][:], bf[1][:], ALU.mult)
            p.wait("sync", ev)
        barrier(p)

    for c in range(8):
        load_fm_chunk(x_in, c, bf[2])
        st = dma(p, "s", xd0[c], bf[2][:])
        p.wait("sync", st)
    barrier(p)
    if upto == "xfm":
        lt = dma(p, "s", bf[2][:], xd0[0])
        p.wait("sync", lt)
        st = dma(p, "s", dbg, bf[2][:])
        p.wait("sync", st)
        barrier(p)
        return p
    if upto == "cols":
        ln_stats._dump_cols = True
    ln_stats(xd0, LN_EPS_PRE)
    if upto == "cols":
        barrier(p)
        return p
    if upto in ("mu", "rstd"):
        src = bf[0] if upto == "mu" else bf[1]
        st = dma(p, "s", dbg, src[:])
        p.wait("sync", st)
        barrier(p)
        return p
    mubc0, rbc0 = bf[5], bf[6]
    vcopy(p, mubc0[:], bf[0][:])
    vcopy(p, rbc0[:], bf[1][:])
    ln_apply_to_bf(xd0)
    if upto == "ln0":
        t = vcopy(p, bf[2][:], bb[XLN[0]][:])
        p.wait("sync", t)
        st = dma(p, "s", dbg, bf[2][:])
        p.wait("sync", st)
        barrier(p)
        return p

    relb = lambda c: bb[RELB[c // 4]][:, 512 * (c % 4):512 * (c % 4 + 1)]
    for c in range(8):
        for i in range(4):
            d1 = dma(p, "s", s_tile[:], rel[128 * i:128 * (i + 1), 128 * c:128 * (c + 1)])
            p.wait("tensor", d1)
            tr = transpose(p, pD[:, 0:128], s_tile[:], s_ident[:])
            p.wait("scalar", tr)
            ev = act(p, relb(c)[:, 128 * i:128 * (i + 1)], pD[:, 0:128], AF.Copy)
            p.wait("tensor", ev)
            p.wait("sync", ev)
    barrier(p)
    posk = lambda oc: bb[POSK[oc // 4]][:, 512 * (oc % 4):512 * (oc % 4 + 1)]
    for oc in range(8):
        wl = dma(p, "g", wbuf[0][:].rearrange("q (c o) -> q c o", c=8),
                 wk.rearrange("(c q) o -> q c o", q=128)[:, :, 128 * oc:128 * (oc + 1)])
        p.wait("tensor", wl)
        for c in range(8):
            t = mm(p, pD[:], wbuf[0][:, 128 * c:128 * (c + 1)], relb(c),
                   start=(c == 0), stop=(c == 7))
        p.wait("scalar", t)
        ev = act(p, posk(oc), pD[:], AF.Identity, bias=s_bias["bk"][:, oc:oc + 1])
        p.wait("tensor", ev)
        p.wait("gpsimd", ev)
    barrier(p)
    if upto == "posk":
        t = act(p, bf[2][:, 0:512], posk(0), AF.Copy)
        t2 = act(p, bf[2][:, 512:1024], posk(1), AF.Copy)
        p.wait("sync", t); p.wait("sync", t2)
        st = dma(p, "s", dbg[:, 0:1024], bf[2][:, 0:1024])
        p.wait("sync", st)
        barrier(p)
        return p

    def qkv_pair(pr):
        t = None
        for (wsrc, wdst, bname, dstpool) in ((wq, wbuf[0], "bq", QP), (wk, wbuf[1], "bk", KP),
                                             (wv, wbuf[2], "bv", VP)):
            wl = dma(p, "g", wdst[:].rearrange("q (c o) -> q c o", c=8),
                     wsrc.rearrange("(c q) o -> q c o", q=128)[:, :, 128 * pr:128 * (pr + 1)])
            p.wait("tensor", wl)
            for nb in range(4):
                for c in range(8):
                    t = mm(p, pA[:, 512 * nb:512 * (nb + 1)], wdst[:, 128 * c:128 * (c + 1)],
                           bb[XLN[c]][:, 512 * nb:512 * (nb + 1)], start=(c == 0), stop=(c == 7))
            p.wait("scalar", t)
            ev = act(p, bb[dstpool][:], pA[:], AF.Identity, bias=s_bias[bname][:, pr:pr + 1])
            p.wait("tensor", ev)
            p.wait("gpsimd", ev)

    def head(h):
        pr, lo = h // 2, (h % 2) * 64
        if lo == 64:
            p.waits("gpsimd", [p.tok("t"), p.tok("a"), p.tok("v")])
            for pool in (QP, KP, VP):
                dd = dma(p, "g", bb[pool][0:64, :], bb[pool][64:128, :])
                p.wait("tensor", dd); p.wait("scalar", dd); p.wait("vector", dd)
        qs = lambda c0, c1: bb[QP][0:64, c0:c1]
        ks = lambda c0, c1: bb[KP][0:64, c0:c1]
        vs = lambda c0, c1: bb[VP][0:64, c0:c1]
        pos_tile = bb[POSK[h // 8]][:, 512 * ((h % 8) // 2):512 * ((h % 8) // 2 + 1)]
        for kc in range(4):
            tr = transpose(p, pC[:, 0:64], pos_tile[lo:lo + 64, 128 * kc:128 * (kc + 1)],
                           s_identb[lo:lo + 64, lo:lo + 64])
            p.wait("vector", tr)
            ev = vcopy(p, s_pkt[kc][:, 0:64], pC[:, 0:64])
            p.wait("tensor", ev)
        for (seld, dstpool) in ((selc, PWC), (selc2, PW2C)):
            t = None
            for kc in range(4):
                sl = dma(p, "s", selbuf[:], seld[128 * kc:128 * (kc + 1), :])
                p.wait("tensor", sl)
                for nb in range(3):
                    pview = (pB[0:64, 512 * nb:512 * (nb + 1)] if nb < 2 else pD[0:64, :])
                    t = mm(p, pview, s_pkt[kc][:, 0:64], selbuf[:, 512 * nb:512 * (nb + 1)],
                           start=(kc == 0), stop=(kc == 3))
                p.wait("sync", t)
            p.wait("scalar", t)
            ev = None
            for nb in range(3):
                pview = (pB[0:64, 512 * nb:512 * (nb + 1)] if nb < 2 else pD[0:64, :])
                ev = act(p, bb[dstpool][0:64, 512 * nb:512 * (nb + 1)], pview, AF.Copy)
            p.wait("tensor", ev)
        p.wait("vector", p.tok("a"))
        c1 = vcopy(p, s_posta[:, 0:1], bb[PWC][0:64, SELW - 1:SELW])
        c2 = vcopy(p, s_posta[:, 1:2], bb[PWC][0:64, 0:1])
        p.wait("tensor", c2)
        for (cidx, which, dst_tile) in ((0, "q", s_a0), (1, "q", s_a511), (0, "k", s_pm0), (1, "k", s_pm511)):
            for nb in range(4):
                rsl = (qs if which == "q" else ks)(512 * nb, 512 * (nb + 1))
                t = mm(p, pD[0:1, :], s_posta[:, cidx:cidx + 1], rsl)
                p.wait("vector", t)
                ev = vcopy(p, dst_tile[0:1, 512 * nb:512 * (nb + 1)], pD[0:1, :])
                p.wait("tensor", ev)
        cb_wr_toks = []
        for qb in range(16):
            ev = None
            for nb in range(3):
                pview = (pB[:, 512 * nb:512 * (nb + 1)] if nb < 2 else pD[:])
                t = mm(p, pview, qs(128 * qb, 128 * (qb + 1)), bb[PWC][0:64, 512 * nb:512 * (nb + 1)])
                p.wait("scalar", t)
                ev = act(p, bb[CHROW][:, 512 * nb:512 * (nb + 1)], pview, AF.Copy)
                p.wait("tensor", ev)
            p.wait("gpsimd", ev)
            wr = dma(p, "g", cb_d[qb, 0:128 * SELW].rearrange("(q f) -> q f", f=SELW), bb[CHROW][:, 0:SELW])
            cb_wr_toks.append(wr)
            p.wait("scalar", wr)
        p.waits("gpsimd", cb_wr_toks)
        ctx_started = [False] * 4
        prev_strip_t = None
        for kb in range(16):
            k0 = 128 * kb
            qlo = max(0, 128 * (kb - 5)); qhi = min(S, 128 * (kb + 6)); bw = qhi - qlo
            band_qbs = list(range(qlo // 128, qhi // 128))
            if prev_strip_t is not None:
                p.wait("gpsimd", prev_strip_t)
            sk_toks = {}
            for j, qb in enumerate(band_qbs):
                off = 767 - 128 * (qb - kb)
                sk_toks[qb] = dma(p, "g", s_sk[j][:],
                                  cb_d[qb, off:off + 128 * (SELW - 1)]
                                  .rearrange("(a b) -> a b", b=SELW - 1)[0:128, 0:128])
            ev = None
            for nb in range(3):
                pview = (pB[:, 512 * nb:512 * (nb + 1)] if nb < 2 else pD[:])
                t = mm(p, pview, ks(k0, k0 + 128), bb[PW2C][0:64, 512 * nb:512 * (nb + 1)])
                p.wait("scalar", t)
                ev = act(p, bb[PHAT][:, 512 * nb:512 * (nb + 1)], pview, AF.Copy)
                p.wait("tensor", ev)
            p.wait("gpsimd", ev)
            wr = dma(p, "g", pb_d[0:128 * SELW].rearrange("(q f) -> q f", f=SELW), bb[PHAT][:, 0:SELW])
            p.wait("gpsimd", wr)
            p.wait("scalar", wr)
            off2 = 767 - k0 + qlo
            skr = dma(p, "g", bb[P2CSK][:, 0:bw],
                      pb_d[off2:off2 + 128 * (SELW - 1)]
                      .rearrange("(a b) -> a b", b=SELW - 1)[0:128, 0:bw])
            p.wait("vector", skr)
            tr = transpose(p, pC[:, 0:64], vs(k0, k0 + 128), s_identb[0:64, 0:64])
            p.wait("vector", tr)
            evt = vcopy(p, s_vt[:, 0:64], pC[:, 0:64])
            p.wait("tensor", evt)
            for hf in range(2):
                h0, h1 = 1024 * hf, 1024 * (hf + 1)
                b0 = min(max(qlo, h0), h1)
                b1 = min(max(qhi, h0), h1)
                bank_ops = {0: [], 1: []}
                for nb in range(2):
                    c0 = h0 + 512 * nb
                    bank_ops[nb].append(("qk", c0, c0 + 512))
                for (t0, t1, left) in ((h0, min(qlo, h1), True), (max(qhi, h0), h1, False)):
                    cpos = t0
                    while cpos < t1:
                        cend = min(t1, (cpos // 512 + 1) * 512)
                        bank_ops[(cpos - h0) // 512].append(("tail", cpos, cend, left))
                        cpos = cend
                for nb in (0, 1):
                    ops = bank_ops[nb]
                    assert ops[0][0] == "qk"
                    for o in ops:
                        if o[0] == "qk":
                            t = mm(p, pB[:, o[1] - h0:o[2] - h0], ks(k0, k0 + 128), qs(o[1], o[2]),
                                   start=True, stop=True, skip_group_check=True)
                        else:
                            _, cpos, cend, left = o
                            pv = pB[:, cpos - h0:cend - h0]
                            rowA = s_a0 if left else s_a511
                            rowP = s_pm0 if left else s_pm511
                            mm(p, pv, s_or_b[0:1, 0:128], rowA[0:1, cpos:cend],
                               start=False, stop=True, skip_group_check=True)
                            t = mm(p, pv, rowP[0:1, k0:k0 + 128], s_or_b[0:1, 0:cend - cpos],
                                   start=False, stop=True, skip_group_check=True)
                eb = None
                if b1 > b0:
                    tr = None
                    p.waits("tensor", list(sk_toks.values()))
                    for qb in range(b0 // 128, b1 // 128):
                        tr = transpose(p, pC[:, (128 * qb - h0):(128 * qb - h0) + 128],
                                       s_sk[band_qbs.index(qb)][:], s_identb[:])
                    p.wait("vector", tr)
                    p.wait("vector", t)
                    tt(p, bf[2][:, 0:b1 - b0], pC[:, b0 - h0:b1 - h0], bb[P2CSK][:, b0 - qlo:b1 - qlo], ALU.add)
                    sv = stt(p, bf[3][:, 0:b1 - b0], pB[:, b0 - h0:b1 - h0], 1.0, bf[2][:, 0:b1 - b0],
                             op0=ALU.mult, op1=ALU.add)
                    p.wait("scalar", sv)
                    eb = act(p, bb[PTT][:, b0:b1], bf[3][:, 0:b1 - b0], AF.Exp, scale=INV_SCALE)
                p.wait("scalar", t)
                for (t0, t1) in ((h0, b0), (b1, h1)):
                    if t1 > t0:
                        eb = act(p, bb[PTT][:, t0:t1], pB[:, t0 - h0:t1 - h0], AF.Exp, scale=INV_SCALE)
                p.waits("tensor", [eb, p.tok("v")])
                for nb in range(2):
                    c0 = h0 + 512 * nb
                    bi = c0 // 512
                    t2 = mm(p, pA[0:65, c0:c0 + 512], s_vt[:], bb[PTT][:, c0:c0 + 512],
                            start=not ctx_started[bi], stop=(kb == 15))
                    ctx_started[bi] = True
                p.waits("tensor", [p.tok("a"), p.tok("v")])
            prev_strip_t = p.tok("t")
        p.wait("scalar", p.tok("t"))
        e1 = act(p, bf[2][64:65, :], pA[64:65, :], AF.Copy)
        e2 = act(p, bb[SCR][0:64, :], pA[0:64, :], AF.Copy)
        p.wait("sync", e1)
        d1 = dma(p, "s", lrow_d[h].unsqueeze(0), bf[2][64:65, :])
        p.wait("sync", e2)
        d2 = dma(p, "s", ctx_d[pr, lo:lo + 64, :], bb[SCR][0:64, :])
        p.waits("tensor", [d1, d2])
        p.waits("scalar", [d1, d2])
        p.waits("vector", [d1, d2])

    for h in range(NH):
        if h % 2 == 0:
            qkv_pair(h // 2)
        head(h)
    barrier(p)
    if upto == "attn_raw":
        t = dma(p, "g", bf[2][:], ctx_d[0])
        p.wait("sync", t)
        st = dma(p, "s", dbg, bf[2][:])
        p.wait("sync", st)
        barrier(p)
        return p

    for pr2 in range(8):
        wl = dma(p, "g", wob[pr2], wo[128 * pr2:128 * (pr2 + 1), :])
    for _tk in p.all_dma_toks():
        p.waits("tensor", [_tk]); p.waits("vector", [_tk])
    for pr2 in range(8):
        for i in range(2):
            h = 2 * pr2 + i
            lt = dma(p, "s", s_row1[0:1, :], lrow_d[h].unsqueeze(0))
            p.wait("sync", lt)
            d2c = row_to_cols(s_row1, 0)
            p.wait("scalar", d2c)
            lr1 = act(p, c_y[:], s_cols[:, 0:16], AF.Ln)
            lr2 = act(p, c_r[:], c_y[:], AF.Exp, scale=-1.0)
            p.wait("vector", lr2)
            rv = vcopy(p, s_cols[:, 16:32], c_r[:])
            p.wait("sync", rv)
            cols_to_row(16, bf[i][0:1, :])
        ld2 = dma(p, "s", bb[XLN[pr2]][:], ctx_d[pr2])
        p.waits("tensor", p.all_dma_toks())
        t = None
        for nb2 in range(4):
            sl2 = slice(512 * nb2, 512 * (nb2 + 1))
            mm(p, pA[:, sl2], s_s64a[:], bf[0][0:1, sl2], start=True, stop=False)
            t = mm(p, pA[:, sl2], s_s64b[:], bf[1][0:1, sl2], start=False, stop=True)
        p.wait("vector", t)
        p.wait("vector", ld2)
        nt = tt(p, bb[XLN[pr2]][:], bb[XLN[pr2]][:], pA[:], ALU.mult)
        p.wait("tensor", nt)
        p.wait("sync", nt)
    barrier(p)
    for oc in range(8):
        for tb in range(4):
            pv = pB[:, 512 * (tb % 2):512 * (tb % 2 + 1)]
            for pr2 in range(8):
                t = mm(p, pv, wob[pr2][:, 128 * oc:128 * (oc + 1)],
                       bb[XLN[pr2]][:, 512 * tb:512 * (tb + 1)], start=(pr2 == 0), stop=(pr2 == 7))
            p.wait("scalar", t)
            ev = act(p, bf[4][:, 512 * tb:512 * (tb + 1)], pv, AF.Identity, bias=s_bias["bo"][:, oc:oc + 1])
            p.wait("tensor", ev)
        lt = dma(p, "s", bf[2][:], xd0[oc])
        p.wait("vector", lt)
        p.wait("vector", ev)
        tt(p, bf[3][:], bf[2][:], mubc0[:], ALU.subtract)
        tt(p, bf[3][:], bf[3][:], rbc0[:], ALU.mult)
        av = tt(p, bf[4][:], bf[4][:], bf[3][:], ALU.add)
        p.wait("sync", av)
        st = dma(p, "s", ao_d[oc], bf[4][:])
        p.wait("scalar", st)
        p.wait("vector", st)
    barrier(p)
    ln_stats(ao_d, LN_EPS_ATT)
    for c in range(8):
        lt = dma(p, "s", bf[2][:], ao_d[c])
        lt2 = dma(p, "s", bf[3][:], xd0[c])
        p.waits("vector", [lt, lt2])
        tt(p, bf[4][:], bf[2][:], bf[0][:], ALU.subtract)
        tt(p, bf[4][:], bf[4][:], bf[1][:], ALU.mult)
        p.op("vector", lambda e, c=c: e.tensor_scalar(out=bf[4][:], in0=bf[4][:],
             scalar1=s_bias["g"][:, c:c + 1], scalar2=s_bias["be"][:, c:c + 1],
             op0=ALU.mult, op1=ALU.add), inc="v")
        av = tt(p, bf[4][:], bf[4][:], bf[3][:], ALU.add)
        p.wait("sync", av)
        st = dma(p, "s", xd1[c], bf[4][:])
        p.wait("vector", st)
    barrier(p)
    if upto == "x1":
        lt = dma(p, "s", bf[2][:], xd1[0])
        p.wait("sync", lt)
        st = dma(p, "s", dbg, bf[2][:])
        p.wait("sync", st)
        barrier(p)
        return p

    ln_stats(xd1, LN_EPS_PRE)
    ln_apply_to_bf(xd1)
    if upto == "x1ln":
        cv = vcopy(p, bf[2][:], bb[XLN[0]][:])
        p.wait("sync", cv)
        st = dma(p, "s", dbg, bf[2][:])
        p.wait("sync", st)
        barrier(p)
        return p
    t1r = [s_a0, s_a511, s_pm0, s_pm511]
    s_evr = [p.sb(f"s_evr{i}", (1, 1024), BF16) for i in range(4)]
    s_eit = p.sb("s_eit", (128, 4), BF16)
    s_lhs1 = p.sb("s_lhs1", (1, 128), BF16)
    s_ujt = [p.sb(f"s_ujt{j}", (4, 512), BF16) for j in range(4)]
    s_uri = [p.sb(f"s_uri{i}", (4, 512), BF16) for i in range(4)]
    for i in range(4):
        for nb in range(4):
            for c in range(8):
                t = mm(p, pD[0:1, :], bb[XLN[c]][:, 2 + 2 * i:3 + 2 * i], bb[XLN[c]][:, 512 * nb:512 * (nb + 1)],
                       start=(c == 0), stop=(c == 7))
            p.wait("vector", t)
            ev = vcopy(p, t1r[i][0:1, 512 * nb:512 * (nb + 1)], pD[0:1, :])
            p.wait("tensor", ev)
    for i in range(4):
        for (col, dest) in ((1 + 2 * i, "m1"), (9 + 2 * i, "m2")):
            for c in range(8):
                tr = transpose(p, pC[0:1, 0:128], bb[XLN[c]][:, col:col + 1], s_identb[:])
                p.wait("vector", tr)
                ev = vcopy(p, (s_evr[i] if dest == "m1" else s_er[0])[0:1, 128 * c:128 * (c + 1)], pC[0:1, 0:128])
                p.wait("tensor", ev)
            tgt = s_evr[i] if dest == "m1" else s_er[0]
            sv = ts(p, tgt[0:1, :], tgt[0:1, :], s_scal[0:1, 0:1])
            if dest == "m2":
                p.wait("sync", sv)
                wr = dma(p, "s", er_d[4 + i].unsqueeze(0), s_er[0][0:1, :])
                p.wait("vector", wr)
    for oc in range(8):
        i = oc // 2
        rd = dma(p, "s", s_eit[:], er_d[4 + i, 512 * (oc % 2):512 * (oc % 2 + 1)].rearrange("(r j) -> r j", j=4))
        p.wait("tensor", rd)
        tr = transpose(p, pC[0:4, 0:128], s_eit[:], s_identb[:])
        p.wait("vector", tr)
        ev = vcopy(p, s_ei[oc][0:4, :], pC[0:4, 0:128])
        p.wait("tensor", ev)
        p.wait("sync", ev)
    barrier(p)
    HQ = lambda hc: bb[10 + hc // 4][:, 512 * (hc % 4):512 * (hc % 4 + 1)]
    for tq in range(4):
        tsl = slice(512 * tq, 512 * (tq + 1))
        for hc in range(32):
            wl = dma(p, "g", wbuf[0][:].rearrange("q (c o) -> q c o", c=8),
                     w1t.rearrange("(c q) o -> q c o", q=128)[:, :, 128 * hc:128 * (hc + 1)])
            p.wait("tensor", wl)
            i = hc // 8
            t = mm(p, pD[:], s_evr[i][0:1, 128 * (hc % 8):128 * (hc % 8 + 1)], t1r[i][0:1, tsl],
                   start=True, stop=False, skip_group_check=True)
            for c in range(8):
                t = mm(p, pD[:], wbuf[0][:, 128 * c:128 * (c + 1)], bb[XLN[c]][:, tsl],
                       start=False, stop=(c == 7), skip_group_check=True)
            p.wait("scalar", t)
            ev = act(p, HQ(hc), pD[:], AF.Gelu, bias=s_b1[:, hc:hc + 1])
            p.wait("tensor", ev)
            p.wait("gpsimd", ev)
        p.wait("tensor", p.tok("a"))
        for j in range(4):
            for c2 in range(8):
                t = mm(p, pD[0:4, :], bb[XLN[c2]][:, 10:18:2], HQ(8 * j + c2),
                       start=(c2 == 0), stop=(c2 == 7))
            p.wait("vector", t)
            ev = vcopy(p, s_ujt[j][0:4, :], pD[0:4, :])
            p.wait("tensor", ev)
            p.wait("sync", ev)
        for i in range(4):
            for j in range(4):
                dma(p, "s", s_uri[i][j:j + 1, :], s_ujt[j][i:i + 1, :])
        p.waits("tensor", p.all_dma_toks())
        for oc in range(8):
            wl1 = dma(p, "g", bb[18][:].rearrange("q (c o) -> q c o", c=16),
                      w2t.rearrange("(c q) o -> q c o", q=128)[:, 0:16, 128 * oc:128 * (oc + 1)])
            wl2 = dma(p, "g", bb[19][:].rearrange("q (c o) -> q c o", c=16),
                      w2t.rearrange("(c q) o -> q c o", q=128)[:, 16:32, 128 * oc:128 * (oc + 1)])
            p.waits("tensor", [wl1, wl2])
            i = oc // 2
            t = mm(p, pB[:, 0:512], s_ei[oc][0:4, :], s_uri[i][0:4, :],
                   start=True, stop=False, skip_group_check=True)
            for c in range(32):
                wtile = bb[18] if c < 16 else bb[19]
                t = mm(p, pB[:, 0:512], wtile[:, 128 * (c % 16):128 * (c % 16 + 1)], HQ(c),
                       start=False, stop=(c == 31), skip_group_check=True)
            p.wait("scalar", t)
            ev = act(p, bf[4][:, 0:512], pB[:, 0:512], AF.Identity, bias=s_bias["b2"][:, oc:oc + 1])
            lt = dma(p, "s", bf[2][:, 0:512], xd1[oc][:, tsl])
            p.waits("vector", [ev, lt])
            av = tt(p, bf[4][:, 0:512], bf[4][:, 0:512], bf[2][:, 0:512], ALU.add)
            p.wait("sync", av)
            st = dma(p, "s", xd0[oc][:, tsl], bf[4][:, 0:512])
            p.wait("tensor", st)
            p.wait("scalar", st)
            p.wait("vector", st)
            p.wait("gpsimd", st)
    barrier(p)

    if upto == "x2c":
        lt = dma(p, "s", bf[2][:], xd0[0])
        p.wait("sync", lt)
        st = dma(p, "s", dbg, bf[2][:])
        p.wait("sync", st)
        barrier(p)
        return p
    for c in range(8):
        lt = dma(p, "s", bf[2][:], xd0[c])
        p.wait("tensor", lt)
        for i in range(16):
            tr = transpose(p, pD[:, 0:128], bf[2][:, 128 * i:128 * (i + 1)], s_ident[:])
            p.wait("vector", tr)
            ev = vcopy(p, s_tile[:], pD[:, 0:128])
            p.wait("sync", ev)
            st = dma(p, "s", x_out[128 * i:128 * (i + 1), 128 * c:128 * (c + 1)], s_tile[:])
            p.wait("tensor", st)
            p.wait("vector", st)
    barrier(p)
    return p

def _fmv(v):
    return np.ascontiguousarray(v.reshape(8, 128).T.astype(np.float32))

def l0_inputs(ins):
    ident = np.eye(128, dtype=np.float32)
    common = dict(
        wq=ins["L0_attn_Wq"], wk=ins["L0_attn_Wk"], wv=ins["L0_attn_Wv"], wo=ins["L0_attn_Wo"],
        bq_fm=_fmv(ins["L0_attn_bq"]), bk_fm=_fmv(ins["L0_attn_bk"]), bv_fm=_fmv(ins["L0_attn_bv"]),
        bo_fm=_fmv(ins["L0_attn_bo"]), g_fm=_fmv(ins["L0_attn_ln_scale"]), be_fm=_fmv(ins["L0_attn_ln_bias"]),
        b2_fm=_fmv(ins["L0_ffn_b2"]),
        rel=ins["L0_attn_rel_embeddings"],
        selc=SELC, selc2=SELC2,
        identb=ident.astype(BF), identf=ident,
        onesrow_b=np.ones((1, 512), dtype=BF), onescol_b=np.ones((128, 1), dtype=BF),
        onesrow_f=np.ones((1, 128), dtype=np.float32),
        sel64a=np.concatenate([np.ones(64), np.zeros(64)]).astype(np.float32).reshape(1, 128),
        sel64b=np.concatenate([np.zeros(64), np.ones(64)]).astype(np.float32).reshape(1, 128),
        w1t=np.ascontiguousarray(ins["L0_ffn_W1"].T.astype(np.float32)),
        b1_fm=np.ascontiguousarray(ins["L0_ffn_b1"].reshape(32, 128).T.astype(np.float32)),
        w2t=np.ascontiguousarray(ins["L0_ffn_W2"].T.astype(np.float32)),
        scal_sc=np.asarray(ins["L0_ffn_scale"], np.float32).reshape(1, 1),
    )
    common = {k: np.ascontiguousarray(np.asarray(v)) for k, v in common.items()}
    maps = []
    for b in range(8):
        m = dict(common)
        m["x_in"] = np.ascontiguousarray(ins["L0_attn_hidden_states"][b].astype(np.float32))
        maps.append(m)
    return maps

class _Row:
    def __init__(self, t): self.t = t
    def __getitem__(self, idx): return self.t[idx]

def build_l1(upto="full"):
    p = P()
    x_in = p.dram("x_in", (S, D), F32, kind="ExternalInput")
    wqkvT = p.dram("wqkvT", (D, 3 * D), F32, kind="ExternalInput")
    bqkv_fm = p.dram("bqkv_fm", (128, 24), F32, kind="ExternalInput")
    wprojT = p.dram("wprojT", (D, D), F32, kind="ExternalInput")
    bproj_fm = p.dram("bproj_fm", (128, 8), F32, kind="ExternalInput")
    rwT = p.dram("rwT", (D, 32), F32, kind="ExternalInput")
    rbias_row = p.dram("rbias_row", (1, 32), F32, kind="ExternalInput")
    idx_row = p.dram("idx_row", (1, 32), F32, kind="ExternalInput")
    gwT = p.dram("gwT", (32, D, 512), F32, kind="ExternalInput")
    uwT = p.dram("uwT", (32, D, 512), F32, kind="ExternalInput")
    dwT = p.dram("dwT", (32, 512, D), F32, kind="ExternalInput")
    sgwT = p.dram("sgwT", (D, D), F32, kind="ExternalInput")
    suwT = p.dram("suwT", (D, D), F32, kind="ExternalInput")
    sdwT = p.dram("sdwT", (D, D), F32, kind="ExternalInput")
    identf = p.dram("identf", (128, 128), F32, kind="ExternalInput")
    identb = p.dram("identb", (128, 128), BF16, kind="ExternalInput")
    onesrow_f = p.dram("onesrow_f", (1, 128), F32, kind="ExternalInput")

    x_out = p.dram("x_out", (S, D), F32, kind="ExternalOutput")
    dbg = p.dram("dbg", (128, 2048), F32, kind="ExternalOutput")

    xd0 = p.dram("xd0", (8, 128, S), F32)
    xd1 = p.dram("xd1", (8, 128, S), F32)
    cfm_d = p.dram("cfm_d", (8, 128, S), BF16)
    row_d = p.dram("row_d", (4096,), F32)
    mrow_d = p.dram("mrow_d", (32, S), F32)

    NBF = 20
    bb = [p.sb(f"bb{i}", (128, 2048), BF16) for i in range(NBF)]
    NF = 7
    bf = [p.sb(f"bf{i}", (128, 2048), F32) for i in range(NF)]
    wbuf = [p.sb(f"wb{i}", (128, 1024), BF16) for i in range(2)]
    s_ident = p.sb("s_ident", (128, 128), F32)
    s_identb = p.sb("s_identb", (128, 128), BF16)
    s_or_f = p.sb("s_or_f", (1, 128), F32)
    s_bqkv = p.sb("s_bqkv", (128, 24), F32)
    s_bproj = p.sb("s_bproj", (128, 8), F32)
    s_tile = p.sb("s_tile", (128, 128), F32)
    s_tileb = p.sb("s_tileb", (128, 128), BF16)
    s_att = p.sb("s_att", (128, 256), F32)
    s_l = p.sb("s_l", (128, 16), F32)
    s_li = p.sb("s_li", (128, 16), F32)
    s_g4 = p.sb("s_g4", (128, 4), F32)
    s_g4b = p.sb("s_g4b", (128, 4), F32)
    s_gm = p.sb("s_gm", (128, 4), F32)
    s_c1 = p.sb("s_c1", (128, 1), F32)
    s_m32 = p.sb("s_m32", (128, 32), F32)
    s_t32 = p.sb("s_t32", (128, 32), F32)
    s_e32 = p.sb("s_e32", (128, 32), F32)
    s_wk = p.sb("s_wk", (128, 4), F32)
    s_M = p.sb("s_M", (128, 32), F32)
    s_Mr = p.sb("s_Mr", (32, 2048), F32)
    s_bias32 = p.sb("s_bias32", (128, 32), F32)
    s_idx32 = p.sb("s_idx32", (128, 32), F32)
    s_sq = None
    s_row1 = p.sb("s_row1", (1, 2048), F32)
    s_mrow = p.sb("s_mrow", (1, 2048), F32)
    onescol_b_t = p.sb("onescol_b_t", (128, 1), BF16)
    c_y = p.sb("c_y", (128, 16), F32)
    c_r = p.sb("c_r", (128, 16), F32)
    c_sum = p.sb("c_sum", (128, 16), F32)
    c_sq = p.sb("c_sq", (128, 16), F32)
    c_mu = p.sb("c_mu", (128, 16), F32)
    c_a = p.sb("c_a", (128, 16), F32)
    s_cols = p.sb("s_cols", (128, 80), F32)

    pA = p.ps("pA", (128, 2048), F32)
    pB = p.ps("pB", (128, 1024), F32)
    pC = p.ps("pC", (128, 1024), BF16)
    pD = p.ps("pD", (128, 512), F32)

    XLN = list(range(0, 8))
    qtile = lambda b: bb[8 + b // 2][:, 1024 * (b % 2):1024 * (b % 2 + 1)]
    ktile = lambda b: bb[12 + b // 2][:, 1024 * (b % 2):1024 * (b % 2 + 1)]
    vtile = lambda b: bb[16 + b // 2][:, 1024 * (b % 2):1024 * (b % 2 + 1)]

    for (dst, src) in [(s_ident[:], identf), (s_identb[:], identb), (s_or_f[:], onesrow_f),
                       (s_bqkv[:], bqkv_fm), (s_bproj[:], bproj_fm)]:
        dma(p, "s", dst, src)
    p.op("vector", lambda e: e.memset(onescol_b_t[:], 1.0), inc="v")
    barrier(p)

    rowd_last = [None]

    def row_to_cols(row_sb, col_off):
        p.wait("sync", rowd_last[0])
        d1 = dma(p, "s", row_d[0:2048].unsqueeze(0), row_sb[0:1, :])
        p.wait("sync", d1)
        d2 = dma(p, "s", s_cols[:, col_off:col_off + 16], row_d[0:2048].rearrange("(q i) -> q i", i=16))
        rowd_last[0] = d2
        return d2

    def cols_to_row(col_off, row_sb):
        p.wait("sync", rowd_last[0])
        d1 = dma(p, "s", row_d[0:2048].rearrange("(q i) -> q i", i=16), s_cols[:, col_off:col_off + 16])
        p.wait("sync", d1)
        d2 = dma(p, "s", row_sb[0:1, :], row_d[0:2048].unsqueeze(0))
        rowd_last[0] = d2
        return d2

    def ln_stats(src_d, eps):
        for c in range(8):
            lt = dma(p, "s", bf[2][:], src_d[c])
            p.wait("gpsimd", lt)
            ct = dma(p, "g", bb[XLN[c]][:], bf[2][:])
            p.wait("sync", ct)
        barrier(p)
        t = None
        for tb in range(4):
            for c in range(8):
                t = mm(p, pA[0:1, 512 * tb:512 * (tb + 1)], onescol_b_t[:], bb[XLN[c]][:, 512 * tb:512 * (tb + 1)],
                       start=(c == 0), stop=(c == 7))
        p.wait("scalar", t)
        e1 = act(p, s_row1[0:1, :], pA[0:1, :], AF.Copy)
        p.wait("tensor", e1)
        last = None
        for tb in range(4):
            for c in range(8):
                sq = act(p, bb[19][:, 0:512], bb[XLN[c]][:, 512 * tb:512 * (tb + 1)], AF.Square)
                p.wait("tensor", sq)
                last = mm(p, pA[0:1, 512 * tb:512 * (tb + 1)], onescol_b_t[:], bb[19][:, 0:512],
                          start=(c == 0), stop=(c == 7))
                p.wait("scalar", last)
        e2 = act(p, bf[3][0:1, :], pA[0:1, :], AF.Copy)
        p.wait("sync", e1); p.wait("sync", e2)
        row_to_cols(s_row1, 0)
        d2 = row_to_cols(_Row(bf[3]), 16)
        p.wait("vector", d2)
        vcopy(p, c_sum[:], s_cols[:, 0:16])
        vcopy(p, c_sq[:], s_cols[:, 16:32])
        ts(p, c_mu[:], c_sum[:], 1.0 / D)
        ts(p, c_a[:], c_sq[:], 1.0 / D)
        tt(p, c_y[:], c_mu[:], c_mu[:], ALU.mult)
        tt(p, c_a[:], c_a[:], c_y[:], ALU.subtract)
        tv = ts(p, c_a[:], c_a[:], float(eps), ALU.add)
        p.wait("scalar", tv)
        act(p, c_y[:], c_a[:], AF.Ln)
        rv = act(p, c_r[:], c_y[:], AF.Exp, scale=-0.5)
        p.wait("vector", rv)
        cm = vcopy(p, s_cols[:, 32:48], c_mu[:])
        rv2 = vcopy(p, s_cols[:, 64:80], c_r[:])
        p.wait("sync", rv2); p.wait("sync", cm)
        cols_to_row(32, s_row1)
        d4 = cols_to_row(64, _Row(bf[3]))
        p.wait("tensor", d4)
        for (rowsb, dst) in ((s_row1, bf[0]), (_Row(bf[3]), bf[1])):
            t = None
            for nb in range(4):
                t = mm(p, pA[:, 512 * nb:512 * (nb + 1)], s_or_f[:], rowsb[0:1][:, 512 * nb:512 * (nb + 1)])
            p.wait("vector", t)
            ev = vcopy(p, dst[:], pA[:])
            p.wait("tensor", ev)
        barrier(p)

    def ln_apply(src_d, f32_dst=None):
        for c in range(8):
            lt = dma(p, "s", bf[2][:], src_d[c])
            p.wait("vector", lt)
            tt(p, bf[3][:], bf[2][:], bf[0][:], ALU.subtract)
            ev = tt(p, bf[3][:], bf[3][:], bf[1][:], ALU.mult)
            ev2 = vcopy(p, bb[XLN[c]][:], bf[3][:])
            p.wait("sync", ev2)
            if f32_dst is not None:
                st = dma(p, "s", f32_dst[c], bf[3][:])
                p.wait("vector", st)
                p.wait("sync", st)
        barrier(p)

    for c in range(8):
        for i in range(16):
            d1 = dma(p, "s", s_tile[:], x_in[128 * i:128 * (i + 1), 128 * c:128 * (c + 1)])
            p.wait("tensor", d1)
            tr = transpose(p, pD[:, 0:128], s_tile[:], s_ident[:])
            p.wait("vector", tr)
            ev = vcopy(p, bf[2][:, 128 * i:128 * (i + 1)], pD[:, 0:128])
            p.wait("tensor", ev)
            p.wait("sync", ev)
        st = dma(p, "s", xd0[c], bf[2][:])
        p.wait("vector", st)
    barrier(p)
    ln_stats(xd0, LN_EPS_PRE)
    ln_apply(xd0)

    for sc in range(2):
        for oc in range(24):
            wl = dma(p, "g", wbuf[0][:].rearrange("q (c o) -> q c o", c=8),
                     wqkvT.rearrange("(c q) o -> q c o", q=128)[:, :, 128 * oc:128 * (oc + 1)])
            p.wait("tensor", wl)
            dcol = (oc % 8) * 128
            for b in range(8):
                col0 = 256 * b + 128 * sc
                for c in range(8):
                    t = mm(p, pD[:, 0:128], wbuf[0][:, 128 * c:128 * (c + 1)],
                           bb[XLN[c]][:, col0:col0 + 128], start=(c == 0), stop=(c == 7))
                p.wait("scalar", t)
                ev = act(p, s_tileb[:], pD[:, 0:128], AF.Identity, bias=s_bqkv[:, oc:oc + 1])
                p.wait("tensor", ev)
                tr = transpose(p, pC[:, 0:128], s_tileb[:], s_identb[:])
                p.wait("vector", tr)
                p.wait("scalar", tr)
                dsttile = (qtile(b) if oc < 8 else (ktile(b) if oc < 16 else vtile(b)))
                evt = vcopy(p, dsttile[:, dcol:dcol + 128], pC[:, 0:128])
                p.wait("tensor", evt)
            p.wait("gpsimd", p.tok("t"))
        barrier(p)
        for r in range(8):
            cs = list(range(4)) if r < 4 else list(range(8))
            L = len(cs)
            for j, c in enumerate(cs):
                tt(p, bf[3][:, 0:1024], qtile(r), ktile(c), ALU.mult)
                vred(p, s_att[:, 16 * j:16 * (j + 1)].unsqueeze(2),
                     bf[3][:, 0:1024].rearrange("q (g d) -> q g d", g=16), ALU.add)
            p.wait("scalar", p.tok("v"))
            ex = act(p, s_att[:, 128:128 + 16 * L], s_att[:, 0:16 * L], AF.Exp, scale=float(np.float32(1.0) / np.float32(8.0)))
            p.wait("vector", ex)
            vcopy(p, s_l[:], s_att[:, 128:144])
            for j in range(1, L):
                tt(p, s_l[:], s_l[:], s_att[:, 128 + 16 * j:144 + 16 * j], ALU.add)
            p.wait("scalar", p.tok("v"))
            lv = act(p, s_li[:], s_l[:], AF.Ln)
            lv = act(p, s_li[:], s_li[:], AF.Exp, scale=-1.0)
            p.wait("vector", lv)
            for j in range(L):
                tt(p, s_att[:, 16 * j:16 * (j + 1)], s_att[:, 128 + 16 * j:144 + 16 * j], s_li[:], ALU.mult)
            for j, c in enumerate(cs):
                pn = s_att[:, 16 * j:16 * (j + 1)].unsqueeze(2).broadcast_to((128, 16, 64))
                if j == 0:
                    tt(p, bf[5][:, 0:1024].rearrange("q (g d) -> q g d", g=16),
                       vtile(c).rearrange("q (g d) -> q g d", g=16), pn, ALU.mult)
                else:
                    tt(p, bf[3][:, 0:1024].rearrange("q (g d) -> q g d", g=16),
                       vtile(c).rearrange("q (g d) -> q g d", g=16), pn, ALU.mult)
                    tt(p, bf[5][:, 0:1024], bf[5][:, 0:1024], bf[3][:, 0:1024], ALU.add)
            cv = vcopy(p, qtile(r), bf[5][:, 0:1024])
            p.wait("sync", cv)
        p.wait("tensor", p.tok("v"))
        for c in range(8):
            for b in range(8):
                tr = transpose(p, pC[:, 0:128], qtile(b)[:, 128 * c:128 * (c + 1)], s_identb[:])
                p.wait("vector", tr)
                ev = vcopy(p, s_tileb[:], pC[:, 0:128])
                p.wait("sync", ev)
                wr = dma(p, "s", cfm_d[c][:, 256 * b + 128 * sc:256 * b + 128 * sc + 128], s_tileb[:])
                p.wait("vector", wr)
                p.wait("tensor", wr)
        barrier(p)

    for hf in range(2):
        tsl = slice(1024 * hf, 1024 * (hf + 1))
        for c in range(8):
            lt = dma(p, "s", bb[c][:, 0:1024], cfm_d[c][:, tsl])
        p.waits("tensor", p.all_dma_toks())
        for oc in range(8):
            wl = dma(p, "g", wbuf[0][:].rearrange("q (c o) -> q c o", c=8),
                     wprojT.rearrange("(c q) o -> q c o", q=128)[:, :, 128 * oc:128 * (oc + 1)])
            p.wait("tensor", wl)
            for nb in range(2):
                for c in range(8):
                    t = mm(p, pB[:, 512 * nb:512 * (nb + 1)], wbuf[0][:, 128 * c:128 * (c + 1)],
                           bb[c][:, 512 * nb:512 * (nb + 1)], start=(c == 0), stop=(c == 7))
            p.wait("scalar", t)
            ev = act(p, bf[4][:, 0:1024], pB[:], AF.Identity, bias=s_bproj[:, oc:oc + 1])
            p.wait("tensor", ev)
            lt = dma(p, "s", bf[2][:, 0:1024], xd0[oc][:, tsl])
            p.waits("vector", [ev, lt])
            av = tt(p, bf[4][:, 0:1024], bf[4][:, 0:1024], bf[2][:, 0:1024], ALU.add)
            p.wait("sync", av)
            st = dma(p, "s", xd1[oc][:, tsl], bf[4][:, 0:1024])
            p.wait("scalar", st)
            p.wait("vector", st)
            p.wait("gpsimd", p.tok("t"))
    barrier(p)
    if upto == "x3":
        lt = dma(p, "s", bf[2][:], xd1[0])
        p.wait("sync", lt)
        st = dma(p, "s", dbg, bf[2][:])
        p.wait("sync", st)
        barrier(p)
        return p

    x1ln_d = p.dram("x1ln_d", (8, 128, S), F32)
    ln_stats(xd1, LN_EPS_PRE)
    ln_apply(xd1, f32_dst=x1ln_d)
    BIG = 1.0e30
    s_rw = p.sb("s_rw", (128, 256), F32)
    s_rbr = p.sb("s_rbr", (1, 32), F32)
    s_ixr = p.sb("s_ixr", (1, 32), F32)
    s_w32 = p.sb("s_w32", (128, 32), F32)
    wl = dma(p, "s", s_rw[:].rearrange("q (c o) -> q c o", c=8),
             rwT.rearrange("(c q) o -> q c o", q=128))
    d1 = dma(p, "s", s_rbr[:], rbias_row)
    d2 = dma(p, "s", s_ixr[:], idx_row)
    p.waits("tensor", p.all_dma_toks())
    for (rsrc, dstt) in ((s_rbr, s_bias32), (s_ixr, s_idx32)):
        t = mm(p, pD[:, 0:32], s_or_f[:], rsrc[0:1, :])
        p.wait("vector", t)
        ev = vcopy(p, dstt[:], pD[:, 0:32])
        p.wait("tensor", ev)
    for c in range(8):
        lt = dma(p, "s", bf[2][:], x1ln_d[c])
        p.waits("tensor", [lt])
        for nb in range(4):
            t = mm(p, pA[0:32, 512 * nb:512 * (nb + 1)], s_rw[:, 32 * c:32 * (c + 1)],
                   bf[2][:, 512 * nb:512 * (nb + 1)], start=(c == 0), stop=(c == 7))
        p.wait("sync", t)
    p.wait("scalar", p.tok("t"))
    ev = act(p, bf[4][0:32, :], pA[0:32, :], AF.Sigmoid)
    p.wait("tensor", ev)
    p.wait("vector", ev)
    for tc in range(16):
        tr = transpose(p, pD[:, 0:32], bf[4][0:32, 128 * tc:128 * (tc + 1)], s_ident[0:32, 0:32])
        p.wait("vector", tr)
        vcopy(p, s_m32[:], pD[:, 0:32])
        tt(p, s_t32[:], s_m32[:], s_bias32[:], ALU.add)
        vred(p, s_g4[:].unsqueeze(2), s_t32[:].rearrange("q (g j) -> q g j", g=4), ALU.max)
        tt(p, s_e32[:].rearrange("q (g j) -> q g j", g=4),
           s_t32[:].rearrange("q (g j) -> q g j", g=4),
           s_g4[:].unsqueeze(2).broadcast_to((128, 4, 8)), ALU.is_equal)
        stt(p, s_e32[:], s_e32[:], -BIG, s_t32[:], op0=ALU.mult, op1=ALU.add)
        vred(p, s_g4b[:].unsqueeze(2), s_e32[:].rearrange("q (g j) -> q g j", g=4), ALU.max)
        tt(p, s_g4[:], s_g4[:], s_g4b[:], ALU.add)
        vred(p, s_c1[:].unsqueeze(2), s_g4[:].unsqueeze(1), ALU.max)
        ts(p, s_gm[:], s_g4[:], s_c1[:, 0:1], ALU.is_equal)
        stt(p, s_g4b[:], s_gm[:], -BIG, s_g4[:], op0=ALU.mult, op1=ALU.add)
        vred(p, s_c1[:].unsqueeze(2), s_g4b[:].unsqueeze(1), ALU.max)
        ts(p, s_g4b[:], s_g4b[:], s_c1[:, 0:1], ALU.is_equal)
        tt(p, s_gm[:], s_gm[:], s_g4b[:], ALU.add)
        tt(p, s_t32[:].rearrange("q (g j) -> q g j", g=4),
           s_t32[:].rearrange("q (g j) -> q g j", g=4),
           s_gm[:].unsqueeze(2).broadcast_to((128, 4, 8)), ALU.mult)
        tt(p, s_t32[:], s_t32[:], s_idx32[:], ALU.subtract)
        for k in range(4):
            vred(p, s_c1[:].unsqueeze(2), s_t32[:].unsqueeze(1), ALU.max)
            ts(p, s_att[:, 32 * k:32 * (k + 1)], s_t32[:], s_c1[:, 0:1], ALU.is_equal)
            tt(p, s_w32[:], s_m32[:], s_att[:, 32 * k:32 * (k + 1)], ALU.mult)
            vred(p, s_wk[:, k:k + 1].unsqueeze(2), s_w32[:].unsqueeze(1), ALU.add)
            stt(p, s_w32[:], s_att[:, 32 * k:32 * (k + 1)], -BIG, s_t32[:], op0=ALU.mult, op1=ALU.add)
            vcopy(p, s_t32[:], s_w32[:])
        vred(p, s_c1[:].unsqueeze(2), s_wk[:].unsqueeze(1), ALU.add)
        tv = ts(p, s_c1[:], s_c1[:], 1.0e-20, ALU.add)
        p.wait("scalar", tv)
        act(p, s_c1[:], s_c1[:], AF.Ln)
        lv = act(p, s_c1[:], s_c1[:], AF.Exp, scale=-1.0)
        p.wait("vector", lv)
        p.op("vector", lambda e: e.tensor_scalar(out=s_wk[:], in0=s_wk[:], scalar1=s_c1[:, 0:1],
             scalar2=2.5, op0=ALU.mult, op1=ALU.mult), inc="v")
        for k in range(4):
            ts(p, s_w32[:], s_att[:, 32 * k:32 * (k + 1)], s_wk[:, k:k + 1], ALU.mult)
            if k == 0:
                vcopy(p, s_M[:], s_w32[:])
            else:
                tt(p, s_M[:], s_M[:], s_w32[:], ALU.add)
        p.wait("tensor", p.tok("v"))
        tr2 = transpose(p, pD[0:32, 0:128], s_M[:], s_ident[:])
        p.wait("vector", tr2)
        ev = vcopy(p, s_Mr[0:32, 128 * tc:128 * (tc + 1)], pD[0:32, 0:128])
        p.wait("tensor", ev)
    p.wait("sync", p.tok("v"))
    wr = dma(p, "s", mrow_d, s_Mr[0:32, :])
    p.wait("sync", wr)
    barrier(p)
    if upto == "M":
        cv = vcopy(p, bf[2][0:32, :], s_Mr[0:32, :])
        p.wait("sync", cv)
        st = dma(p, "s", dbg[0:32, :], bf[2][0:32, :])
        p.wait("sync", st)
        barrier(p)
        return p

    def load_pair(dstA, dstB, srcT, ncols):
        wl1 = dma(p, "g", dstA[:].rearrange("q (c o) -> q c o", c=4),
                  srcT.rearrange("(c q) o -> q c o", q=128)[:, 0:4, :])
        wl2 = dma(p, "g", dstB[:].rearrange("q (c o) -> q c o", c=4),
                  srcT.rearrange("(c q) o -> q c o", q=128)[:, 4:8, :])
        return [wl1, wl2]

    acct = lambda oc: bf[3 + oc // 2][:, 1024 * (oc % 2):1024 * (oc % 2 + 1)]
    gtile = lambda c: bb[8 + c // 4][:, 512 * (c % 4):512 * (c % 4 + 1)]
    utile = lambda c: bb[10 + c // 4][:, 512 * (c % 4):512 * (c % 4 + 1)]
    dtile = lambda ic: bb[12 + ic // 2][:, 1024 * (ic % 2):1024 * (ic % 2 + 1)]
    hht = lambda ic: bb[15 + ic // 2][:, 1024 * (ic % 2):1024 * (ic % 2 + 1)]

    for hf in range(2):
        tsl = slice(1024 * hf, 1024 * (hf + 1))
        for i in range(4):
            p.op("vector", lambda e, i=i: e.memset(bf[3 + i][:], 0.0), inc="v")
        for e in range(32):
            toks = load_pair(bb[8], bb[9], gwT[e], 512) + load_pair(bb[10], bb[11], uwT[e], 512)
            toks += [dma(p, "g", bb[12][:].rearrange("q (c o) -> q c o", c=2),
                         dwT[e].rearrange("(c q) o -> q c o", q=128)[:, 0:2, :]),
                     dma(p, "g", bb[13][:].rearrange("q (c o) -> q c o", c=2),
                         dwT[e].rearrange("(c q) o -> q c o", q=128)[:, 2:4, :])]
            ml = dma(p, "s", s_mrow[0:1, :], mrow_d[e].unsqueeze(0))
            p.waits("tensor", toks + [ml])
            for nb in range(2):
                t = mm(p, pA[:, 1024 + 512 * nb:1024 + 512 * (nb + 1)], s_or_f[:],
                       s_mrow[0:1, 1024 * hf + 512 * nb:1024 * hf + 512 * (nb + 1)])
            for ic in range(4):
                for nb in range(2):
                    for c in range(8):
                        mm(p, pB[:, 512 * nb:512 * (nb + 1)], gtile(c)[:, 128 * ic:128 * (ic + 1)],
                           bb[XLN[c]][:, 1024 * hf + 512 * nb:1024 * hf + 512 * (nb + 1)],
                           start=(c == 0), stop=(c == 7))
                    for c in range(8):
                        t = mm(p, pA[:, 512 * nb:512 * (nb + 1)], utile(c)[:, 128 * ic:128 * (ic + 1)],
                               bb[XLN[c]][:, 1024 * hf + 512 * nb:1024 * hf + 512 * (nb + 1)],
                               start=(c == 0), stop=(c == 7))
                p.wait("scalar", t)
                sg = act(p, bb[14][:, 0:1024], pB[:], AF.Silu)
                p.wait("vector", sg)
                tt(p, bb[14][:, 1024:2048], bb[14][:, 0:1024], pA[:, 0:1024], ALU.mult)
                hv = tt(p, hht(ic), bb[14][:, 1024:2048], pA[:, 1024:2048], ALU.mult)
                p.wait("tensor", hv)
            for oc in range(8):
                for nb in range(2):
                    for ic in range(4):
                        t = mm(p, pB[:, 512 * nb:512 * (nb + 1)], dtile(ic)[:, 128 * oc:128 * (oc + 1)],
                               hht(ic)[:, 512 * nb:512 * (nb + 1)], start=(ic == 0), stop=(ic == 3))
                p.wait("vector", t)
                sv = stt(p, acct(oc), pB[:], 1.0, acct(oc), op0=ALU.mult, op1=ALU.add)
                p.wait("tensor", sv)
            p.wait("gpsimd", p.tok("t"))
            p.wait("sync", p.tok("t"))
        sgt = lambda c: bb[8 + c // 2][:, 1024 * (c % 2):1024 * (c % 2 + 1)]
        sut = lambda c: bb[12 + c // 2][:, 1024 * (c % 2):1024 * (c % 2 + 1)]
        hhs = lambda icc: bb[16 + icc // 2][:, 1024 * (icc % 2):1024 * (icc % 2 + 1)]
        sdt = lambda icc: bb[8 + icc // 2][:, 1024 * (icc % 2):1024 * (icc % 2 + 1)]
        toks = []
        for c2 in range(8):
            toks.append(dma(p, "g", sgt(c2)[:, :].rearrange("q (a o) -> q a o", a=1),
                            sgwT.rearrange("(c q) o -> q c o", q=128)[:, c2:c2 + 1, :]))
            toks.append(dma(p, "g", sut(c2)[:, :].rearrange("q (a o) -> q a o", a=1),
                            suwT.rearrange("(c q) o -> q c o", q=128)[:, c2:c2 + 1, :]))
        p.waits("tensor", toks)
        for icc in range(8):
            for nb in range(2):
                for c in range(8):
                    mm(p, pB[:, 512 * nb:512 * (nb + 1)], sgt(c)[:, 128 * icc:128 * (icc + 1)],
                       bb[XLN[c]][:, 1024 * hf + 512 * nb:1024 * hf + 512 * (nb + 1)],
                       start=(c == 0), stop=(c == 7))
                for c in range(8):
                    t = mm(p, pA[:, 512 * nb:512 * (nb + 1)], sut(c)[:, 128 * icc:128 * (icc + 1)],
                           bb[XLN[c]][:, 1024 * hf + 512 * nb:1024 * hf + 512 * (nb + 1)],
                           start=(c == 0), stop=(c == 7))
            p.wait("scalar", t)
            sg = act(p, bf[2][:, 0:1024], pB[:], AF.Silu)
            p.wait("vector", sg)
            hv = tt(p, hhs(icc), bf[2][:, 0:1024], pA[:, 0:1024], ALU.mult)
            p.wait("tensor", hv)
        p.wait("gpsimd", p.tok("t"))
        toks = []
        for c2 in range(8):
            toks.append(dma(p, "g", sdt(c2)[:, :].rearrange("q (a o) -> q a o", a=1),
                            sdwT.rearrange("(c q) o -> q c o", q=128)[:, c2:c2 + 1, :]))
        p.waits("tensor", toks)
        for oc in range(8):
            for nb in range(2):
                for icc in range(8):
                    t = mm(p, pB[:, 512 * nb:512 * (nb + 1)], sdt(icc)[:, 128 * oc:128 * (oc + 1)],
                           hhs(icc)[:, 512 * nb:512 * (nb + 1)], start=(icc == 0), stop=(icc == 7))
            p.wait("vector", t)
            sv = stt(p, acct(oc), pB[:], 1.0, acct(oc), op0=ALU.mult, op1=ALU.add)
            p.wait("tensor", sv)
        p.wait("gpsimd", p.tok("t"))
        p.waits("sync", [p.tok("a"), p.tok("v"), p.tok("t")])
        for oc in range(8):
            lt = dma(p, "s", bf[2][:, 0:1024], xd1[oc][:, tsl])
            p.wait("vector", lt)
            av = tt(p, bf[2][:, 0:1024], bf[2][:, 0:1024], acct(oc), ALU.add)
            p.wait("sync", av)
            st = dma(p, "s", xd0[oc][:, tsl], bf[2][:, 0:1024])
            p.wait("vector", st)
            p.wait("sync", st)
        barrier(p)

    for c in range(8):
        lt = dma(p, "s", bf[2][:], xd0[c])
        p.wait("tensor", lt)
        for i in range(16):
            tr = transpose(p, pD[:, 0:128], bf[2][:, 128 * i:128 * (i + 1)], s_ident[:])
            p.wait("vector", tr)
            ev = vcopy(p, s_tile[:], pD[:, 0:128])
            p.wait("sync", ev)
            st = dma(p, "s", x_out[128 * i:128 * (i + 1), 128 * c:128 * (c + 1)], s_tile[:])
            p.wait("tensor", st)
            p.wait("vector", st)
    barrier(p)
    return p

def l1_inputs(ins):
    ident = np.eye(128, dtype=np.float32)
    common = dict(
        wqkvT=np.ascontiguousarray(ins["L1_attn_Wqkv"].T.astype(np.float32)),
        bqkv_fm=np.ascontiguousarray(ins["L1_attn_bqkv"].reshape(24, 128).T.astype(np.float32)),
        wprojT=np.ascontiguousarray(ins["L1_attn_Wproj"].T.astype(np.float32)),
        bproj_fm=_fmv(ins["L1_attn_bproj"]),
        rwT=np.ascontiguousarray(ins["L1_ffn_router_weight"].T.astype(np.float32)),
        rbias_row=np.ascontiguousarray(ins["L1_ffn_router_bias"].reshape(1, 32).astype(np.float32)),
        idx_row=(np.arange(32, dtype=np.float32) * np.float32(1e-9)).reshape(1, 32),
        gwT=np.ascontiguousarray(ins["L1_ffn_gate_w"].transpose(0, 2, 1).astype(np.float32)),
        uwT=np.ascontiguousarray(ins["L1_ffn_up_w"].transpose(0, 2, 1).astype(np.float32)),
        dwT=np.ascontiguousarray(ins["L1_ffn_down_w"].transpose(0, 2, 1).astype(np.float32)),
        sgwT=np.ascontiguousarray(ins["L1_ffn_shared_gate_w"].T.astype(np.float32)),
        suwT=np.ascontiguousarray(ins["L1_ffn_shared_up_w"].T.astype(np.float32)),
        sdwT=np.ascontiguousarray(ins["L1_ffn_shared_down_w"].T.astype(np.float32)),
        identf=ident, identb=ident.astype(BF),
        onesrow_f=np.ones((1, 128), dtype=np.float32),
    )
    common = {k: np.ascontiguousarray(np.asarray(v)) for k, v in common.items()}
    return common

def l1_maps(ins, x2_full):
    common = l1_inputs(ins)
    maps = []
    for j in range(8):
        m = dict(common)
        m["x_in"] = np.ascontiguousarray(
            x2_full[:, 256 * j:256 * (j + 1), :].reshape(2048, 1024).astype(np.float32))
        maps.append(m)
    return maps

_L0_CACHE = {}

def kernel(**inputs):
    if "l0" not in _L0_CACHE:
        _L0_CACHE["l0"] = build_l0().finalize()
        _L0_CACHE["l1"] = build_l1().finalize()
    nc0 = _L0_CACHE["l0"]
    nc1 = _L0_CACHE["l1"]
    maps0 = l0_inputs(inputs)
    res0 = run_bass_kernel_spmd(nc0, maps0, core_ids=list(range(8)))
    x2 = np.stack([np.asarray(r["x_out"]) for r in res0.results], axis=0)
    maps1 = l1_maps(inputs, x2)
    res1 = run_bass_kernel_spmd(nc1, maps1, core_ids=list(range(8)))
    out = np.empty((B, S, D), dtype=np.float32)
    for j in range(8):
        out[:, 256 * j:256 * (j + 1), :] = np.asarray(res1.results[j]["x_out"]).reshape(8, 256, 1024)
    return out
